# version 11 (final)
# baseline (speedup 1.0000x reference)
import contextlib
import numpy as np
import concourse.bass as bass
import concourse.mybir as mybir
from concourse.bass_utils import run_bass_kernel_spmd

F32 = mybir.dt.float32
BF16 = mybir.dt.bfloat16
I32 = mybir.dt.int32
AF = mybir.ActivationFunctionType
ALU = mybir.AluOpType
AX = mybir.AxisListType.X

B, L, D, DS, HID = 4, 8192, 1024, 16, 4096
EHALF, EB, KB, THL = 512, 4, 8, 4096
W = 1024
NW = L // W
QP = 127.0
LN_EPS = 1e-5
MIN_RANGE = 1e-6

S_TMP, S_SX, S_RSX, S_S1, S_SW1, S_RSW1, S_SW2, S_RSW2, S_S2, S_RS2, S_S2B = range(11)


class Ct:
    def __init__(self, h):
        self.h = h
        self.v = 0

    def take(self, k=1):
        self.v += k
        return self.v


def build(dbg=()):
    nc = bass.Bass("TRN2", target_bir_lowering=False)

    x_h = nc.dram_tensor("x", [L, D], F32, kind="ExternalInput").ap()
    wd_h = nc.dram_tensor("wd", [EHALF, D], F32, kind="ExternalInput").ap()
    wbc_h = nc.dram_tensor("wbc", [2 * DS, D], F32, kind="ExternalInput").ap()
    asl_h = nc.dram_tensor("asl", [EHALF, DS], F32, kind="ExternalInput").ap()
    bdsl_h = nc.dram_tensor("bdsl", [EHALF], F32, kind="ExternalInput").ap()
    bbc_h = nc.dram_tensor("bbc", [2 * DS], F32, kind="ExternalInput").ap()
    w1_h = nc.dram_tensor("w1", [HID, D], F32, kind="ExternalInput").ap()
    b1_h = nc.dram_tensor("b1", [HID], F32, kind="ExternalInput").ap()
    w2_h = nc.dram_tensor("w2", [D, HID], F32, kind="ExternalInput").ap()
    b2_h = nc.dram_tensor("b2", [D], F32, kind="ExternalInput").ap()
    wmod_h = nc.dram_tensor("wmod", [12, KB, 128, 512], F32, kind="ExternalInput").ap()
    crow_h = nc.dram_tensor("crow", [1, D], F32, kind="ExternalInput").ap()
    eye_h = nc.dram_tensor("eye", [128, 128], F32, kind="ExternalInput").ap()
    cfg_h = nc.dram_tensor("cfg", [1, 16], I32, kind="ExternalInput").ap()
    out_h = nc.dram_tensor("out", [THL, D], F32, kind="ExternalOutput").ap()

    bcq_d = nc.dram_tensor("bcq", [2, 2 * DS, W], BF16, kind="Internal")
    xps_d = nc.dram_tensor("xps", [EB, 128, L], BF16, kind="Internal")
    xpsh_d = nc.dram_tensor("xpsh", [EB, 128, THL], BF16, kind="Internal")
    xch_d = nc.dram_tensor("xch", [2, EB, 128, THL], BF16, kind="Internal")
    w1qt_d = nc.dram_tensor("w1qt", [32, KB, 128, 128], BF16, kind="Internal")
    w2qt_d = nc.dram_tensor("w2qt", [KB, 32, 128, 128], BF16, kind="Internal")
    gsp_d = nc.dram_tensor("gsp", [32, 128, THL], BF16, kind="Internal")
    gate_d = nc.dram_tensor("gate", [128, 64], F32, kind="Internal")
    pb_d = nc.dram_tensor("pb", [1, 128], F32, kind="Internal")
    mx1i_d = nc.dram_tensor("mx1i", [1, 8], F32, kind="Internal")
    mx1o_d = nc.dram_tensor("mx1o", [1, 8], F32, kind="Internal")
    mx2i_d = nc.dram_tensor("mx2i", [1, 8], F32, kind="Internal")
    mx2o_d = nc.dram_tensor("mx2o", [1, 8], F32, kind="Internal")
    sml_d = nc.dram_tensor("smld", [1, 16], F32, kind="Internal")

    dbg_h = {k: nc.dram_tensor(k, list(s), F32, kind="ExternalOutput").ap()
             for k, s in dict(dbg).items()}

    with contextlib.ExitStack() as stack:
        def sb(name, shape, dt):
            return stack.enter_context(nc.sbuf_tensor(name, shape, dt))

        def sem(name):
            return Ct(stack.enter_context(nc.semaphore(name)))

        eyef = sb("eyef", [128, 128], F32)
        eyeb = sb("eyeb", [128, 128], BF16)
        ones1 = sb("ones1", [1, 128], BF16)
        onescol = sb("onescol", [128, 1], BF16)
        cfgs = sb("cfgs", [1, 16], I32)
        ca1p = sb("ca1p", [128, KB], F32)
        sap = sb("sap", [128, KB], F32)
        cb1p = sb("cb1p", [128, KB], F32)
        sbp = sb("sbp", [128, KB], F32)
        ganeg = sb("ganeg", [128, EB], F32)
        gbrow = sb("gbrow", [1, D], BF16)
        gbp = sb("gbp", [128, KB], F32)
        gbrep = sb("gbrep", [128, D], BF16)
        sml = sb("sml", [1, 16], F32)
        smlb = sb("smlb", [128, 16], F32)
        b1sb = sb("b1sb", [128, 32], F32)
        b1s1 = sb("b1s1", [128, 32], F32)
        b2row = sb("b2row", [1, D], F32)
        sgbrep = sb("sgbrep", [128, D], BF16)
        c2rep = sb("c2rep", [128, D], BF16)
        gmaxacc = sb("gmaxacc", [128, 4], F32)
        gmx32 = sb("gmx32", [128, 32], F32)
        epsv = sb("epsv", [128, 1], F32)
        srow = sb("srow", [1, 3 * W], F32)
        mix_ps = contextlib.ExitStack()

        def sbm(name, shape, dt):
            return mix_ps.enter_context(nc.sbuf_tensor(name, shape, dt))
        wdt = sbm("wdt", [128, KB * EHALF], BF16)
        wbct = sbm("wbct", [128, KB * 2 * DS], BF16)
        negA = sbm("negA", [128, EB * DS], F32)
        negc1 = sbm("negc1", [128, EB], F32)
        bcb = sbm("bcb", [2 * DS, 1], F32)
        saca = sbm("saca", [128, KB], BF16)

        psU = stack.enter_context(nc.psum_tensor("psU", [128, 1024], F32))
        psV = stack.enter_context(nc.psum_tensor("psV", [128, 1024], F32))
        psS = stack.enter_context(nc.psum_tensor("psS", [128, 512], F32))
        psT = stack.enter_context(nc.psum_tensor("psT", [128, 512], F32))
        psX = stack.enter_context(nc.psum_tensor("psX", [128, 512], F32))
        psY = stack.enter_context(nc.psum_tensor("psY", [128, 512], F32))
        psSb = psS.ap().bitcast(BF16)
        psTb = psT.ap().bitcast(BF16)
        psXb = psX.ap().bitcast(BF16)
        psYb = psY.ap().bitcast(BF16)

        TK = sem("tk")
        sqsc = sem("sqsc")
        sqmm = sem("sqmm")
        dpmm = sem("dpmm")
        dpsc = sem("dpsc")
        sA = sem("sA")
        vA = sem("vA")
        vN = sem("vN")
        bcs = sem("bcs")
        ldx = sem("ldx")
        ftr = sem("ftr")
        fmm = sem("fmm")
        fsc = sem("fsc")
        fmx = sem("fmx")
        fsp = sem("fsp")
        ccs = sem("ccs")

        blk = stack.enter_context(nc.Block())
        REGS = {}

        def step(eng, fn, n_dma=0, extra_waits=()):
            w = TK.v
            TK.take(16 * n_dma if n_dma else 1)
            sec = getattr(blk, eng)

            def body(e, fn=fn, w=w, n_dma=n_dma, extra_waits=tuple(extra_waits)):
                e.wait_ge(TK.h, w)
                for (s_, val) in extra_waits:
                    e.wait_ge(s_.h, val)
                r = fn(e)
                if n_dma:
                    insts = r if isinstance(r, (list, tuple)) else [r]
                    assert len(insts) == n_dma, (len(insts), n_dma)
                    for i in insts:
                        i.then_inc(TK.h, 16)
                else:
                    r.then_inc(TK.h, 1)
            sec(body)

        def dbg_out(key, src_ap):
            if key not in dbg_h:
                return
            step("gpsimd", lambda g, k=key, a=src_ap:
                 g.dma_start(out=dbg_h[k], in_=a), n_dma=1)

        def vsp(v):
            v.tensor_scalar_add(sml[0:1, 15:16], sml[0:1, 14:15], 0.0)
            v.tensor_scalar_add(sml[0:1, 15:16], sml[0:1, 14:15], 0.0)
            v.tensor_scalar_add(sml[0:1, 15:16], sml[0:1, 14:15], 0.0)

        def f(sy):
            return [sy.dma_start(out=cfgs[:], in_=cfg_h[:]),
                    sy.dma_start(out=eyef[:], in_=eye_h[:]),
                    sy.dma_start(out=b1sb[:], in_=b1_h.rearrange("(jb p) -> p jb", p=128), allow_slow_non_contiguous=True),
                    sy.dma_start(out=b2row[:], in_=b2_h.unsqueeze(0)),
                    sy.dma_start(out=srow[0:1, 0:D], in_=crow_h[:]),
                    sy.dma_start(out=negA[:], in_=asl_h.rearrange("(eb p) n -> p eb n", p=128)),
                    sy.dma_start(out=bcb[:, 0:1], in_=bbc_h.unsqueeze(1), allow_slow_non_contiguous=True)]
        step("sync", f, n_dma=7)

        def f(sy):
            REGS["th"] = sy.value_load(cfgs[0:1, 8:9])
            REGS["ag"] = sy.value_load(cfgs[0:1, 9:10])
            REGS["eh"] = sy.value_load(cfgs[0:1, 11:12])
            REGS["inv"] = sy.value_load(cfgs[0:1, 12:13])
            REGS["ds_eh4"] = bass.ts(REGS["eh"], 4)
            REGS["ds_inv4"] = bass.ts(REGS["inv"], 4)
            REGS["ds_invT"] = bass.ts(REGS["inv"], THL)
            REGS["ds_thT"] = bass.ts(REGS["th"], THL)
            REGS["ds_ag"] = bass.ts(REGS["ag"], 1)
            return sy.nop()
        step("sync", f)

        def f(g):
            REGS["g_th"] = g.value_load(cfgs[0:1, 8:9])
            REGS["g_ag"] = g.value_load(cfgs[0:1, 9:10])
            REGS["g_inv"] = g.value_load(cfgs[0:1, 12:13])
            REGS["gds_invT"] = bass.ts(REGS["g_inv"], THL)
            REGS["gds_thT"] = bass.ts(REGS["g_th"], THL)
            REGS["gds_ag"] = bass.ts(REGS["g_ag"], 1)
            return g.nop()
        step("gpsimd", f)

        def f(v):
            v.tensor_copy(eyeb[:], eyef[:])
            v.memset(ones1[:], 1.0)
            v.memset(onescol[:], 1.0)
            v.memset(gmaxacc[:], 0.0)
            v.memset(epsv[:], LN_EPS)
            v.memset(sml[:], 0.0)
            return v.tensor_scalar_mul(negA[:], negA[:], -1.0)
        step("vector", f)

        step("scalar", lambda s: s.activation(srow[0:1, 0:D], srow[0:1, 0:D],
                                              AF.Silu, bias=0.0, scale=1.0))

        with contextlib.ExitStack() as _ps:
            scfm = _ps.enter_context(nc.sbuf_tensor("scfm", [128, KB], F32))
            wmb0 = _ps.enter_context(nc.sbuf_tensor("wmb0", [128, KB * 512], F32))
            wmb1 = _ps.enter_context(nc.sbuf_tensor("wmb1", [128, KB * 512], F32))
            modrow = _ps.enter_context(nc.sbuf_tensor("modrow", [1, 6 * D], F32))
            graw = _ps.enter_context(nc.sbuf_tensor("graw", [128, 48], F32))
            gates2 = _ps.enter_context(nc.sbuf_tensor("gates2", [128, 48], F32))
            gtmp = _ps.enter_context(nc.sbuf_tensor("gtmp", [128, 64], F32))
            _dummy_ = 0
            wmb = [wmb0, wmb1]

            def f(t):
                for kb in range(KB):
                    i = t.transpose(psS[:, kb:kb + 1], srow[0:1, kb * 128:(kb + 1) * 128],
                                    eyef[0:1, 0:1])
                return i
            step("tensor", f)
            step("vector", lambda v: v.tensor_copy(scfm[:], psS[:, 0:KB]))

            for ch in range(12):
                step("gpsimd", lambda g, ch=ch:
                     g.dma_start(out=wmb[ch % 2][:],
                                 in_=wmod_h[ch].rearrange("kb p c -> p kb c")), n_dma=1)

                def f(t, ch=ch):
                    for kb in range(KB):
                        i = t.matmul(psX[0:1, 0:512], scfm[:, kb:kb + 1],
                                     wmb[ch % 2][:, kb * 512:(kb + 1) * 512],
                                     start=(kb == 0), stop=(kb == KB - 1))
                    return i
                step("tensor", f)
                step("scalar", lambda s, ch=ch:
                     s.copy(modrow[0:1, ch * 512:(ch + 1) * 512], psX[0:1, 0:512]))

            dbg_out("d_mod", modrow[0:1, 0:1024])

            def f(t):
                for col in range(48):
                    i = t.transpose(psS[:, col:col + 1],
                                    modrow[0:1, col * 128:(col + 1) * 128], eyef[0:1, 0:1])
                return i
            step("tensor", f)
            step("vector", lambda v: v.tensor_copy(graw[:], psS[:, 0:48]))

            def f(v):
                vsp(v)
                rawA, rawB = graw[:, 16:24], graw[:, 40:48]
                t = [gtmp[:, 8 * k:8 * (k + 1)] for k in range(8)]
                v.tensor_scalar_max(t[0], rawA, 0.25)
                v.tensor_scalar_max(t[1], rawB, 0.25)
                v.tensor_scalar_min(t[2], rawA, -0.25)
                v.tensor_scalar_min(t[3], rawB, -0.25)
                v.tensor_scalar(t[4], rawA, 0.0, None, ALU.is_ge)
                v.tensor_scalar(t[5], rawB, 0.0, None, ALU.is_ge)
                v.tensor_tensor(t[6], t[0], t[2], ALU.subtract)
                v.tensor_tensor(t[7], t[1], t[3], ALU.subtract)
                v.tensor_scalar_add(gates2[:, 0:8], graw[:, 8:16], 1.0)
                v.tensor_scalar_add(gates2[:, 8:16], graw[:, 0:8], 0.0)
                v.tensor_tensor(t[0], t[4], t[6], ALU.mult)
                v.tensor_tensor(t[1], t[5], t[7], ALU.mult)
                v.tensor_scalar_add(gates2[:, 16:24], graw[:, 32:40], 1.0)
                v.tensor_tensor(t[4], t[0], t[2], ALU.add)
                v.tensor_tensor(gates2[:, 40:48], t[1], t[3], ALU.add)
                v.tensor_scalar_add(gates2[:, 24:32], graw[:, 24:32], 0.0)
                i = v.tensor_scalar_mul(gates2[:, 32:40], t[4], -1.0)
                return i
            step("vector", f)
            dbg_out("d_gates", gates2[:, 0:48])

            step("sync", lambda sy: sy.dma_start(out=gate_d.ap()[:, 0:48], in_=gates2[:]),
                 n_dma=1)

            def f(sy):
                gd = gate_d.ap()
                reh, rinv = REGS["eh"], REGS["inv"]
                outs = []
                for (dst, g0) in ((ca1p, 0), (sap, 8), (cb1p, 16), (sbp, 24), (gbp, 40)):
                    gv = gd[:, g0:g0 + 8]
                    outs.append(sy.dma_start(out=dst[:, 0:4], in_=gv[:, REGS["ds_eh4"]]))
                    outs.append(sy.dma_start(out=dst[:, 4:8], in_=gv[:, REGS["ds_inv4"]]))
                outs.append(sy.dma_start(out=ganeg[:], in_=gd[:, 32:40][:, REGS["ds_eh4"]]))
                return outs
            step("sync", f, n_dma=11)

            def f(t):
                for kb in range(KB):
                    ps = psS if kb < 4 else psT
                    i = t.transpose(ps[0:1, (kb % 4) * 128:(kb % 4 + 1) * 128],
                                    gbp[:, kb:kb + 1], eyef[:, :])
                return i
            step("tensor", f)

            def f(v):
                v.tensor_copy(gbrow[0:1, 0:512], psS[0:1, 0:512])
                return v.tensor_copy(gbrow[0:1, 512:1024], psT[0:1, 0:512])
            step("vector", f)

            def f(t):
                t.matmul(psU[:, 0:512], ones1[:], gbrow[0:1, 0:512], start=True, stop=True)
                return t.matmul(psU[:, 512:1024], ones1[:], gbrow[0:1, 512:1024],
                                start=True, stop=True)
            step("tensor", f)
            step("vector", lambda v: v.tensor_copy(gbrep[:], psU[:, 0:1024]))

        with contextlib.ExitStack() as _ps:
            wdcast = _ps.enter_context(nc.sbuf_tensor("wdcast", [128, EB * D], BF16))
            bdn = _ps.enter_context(nc.sbuf_tensor("bdn", [128, EB], F32))
            sacaf = _ps.enter_context(nc.sbuf_tensor("sacaf", [128, KB], F32))
            _dummy_ = 0
            step("sync", lambda sy: sy.dma_start(
                out=bdn[:], in_=bdsl_h.rearrange("(eb p) -> p eb", p=128), allow_slow_non_contiguous=True), n_dma=1)

            def f(v):
                v.reciprocal(sacaf[:], ca1p[:])
                vsp(v)
                v.tensor_tensor(sacaf[:], sacaf[:], sap[:], ALU.mult)
                vsp(v)
                return v.tensor_copy(saca[:], sacaf[:])
            step("vector", f)

            step("gpsimd", lambda g: g.dma_start(
                out=wdcast[:], in_=wd_h.rearrange("(eb p) c -> p eb c", p=128)), n_dma=1)
            for half in range(2):
                def f(t, half=half):
                    for idx in range(16):
                        eb, j = divmod(half * 16 + idx, KB)
                        ps = (psSb, psTb, psXb, psYb)[idx // 4]
                        i = t.transpose(ps[:, (idx % 4) * 128:(idx % 4 + 1) * 128],
                                        wdcast[:, eb * D + j * 128:eb * D + (j + 1) * 128],
                                        eyeb[:, :])
                    return i
                step("tensor", f)

                def f(s, half=half):
                    for idx in range(16):
                        eb, j = divmod(half * 16 + idx, KB)
                        ps = (psSb, psTb, psXb, psYb)[idx // 4]
                        i = s.activation(wdt[:, j * EHALF + eb * 128:j * EHALF + (eb + 1) * 128],
                                         ps[:, (idx % 4) * 128:(idx % 4 + 1) * 128],
                                         AF.Copy, bias=0.0, scale=ca1p[:, j:j + 1])
                    return i
                step("scalar", f)

            def f(t):
                for eb in range(EB):
                    for j in range(KB):
                        i = t.matmul(psT[:, eb:eb + 1],
                                     wdt[:, j * EHALF + eb * 128:j * EHALF + (eb + 1) * 128],
                                     saca[:, j:j + 1], start=(j == 0), stop=(j == KB - 1))
                return i
            step("tensor", f)
            step("vector", lambda v: v.scalar_tensor_tensor(
                negc1[:], psT[:, 0:EB], -1.0, bdn[:], ALU.mult, ALU.subtract))

            step("gpsimd", lambda g: g.dma_start(
                out=wdcast[0:32, 0:D], in_=wbc_h[:, :]), n_dma=1)

            def f(t):
                for j in range(KB):
                    i = t.transpose(psSb[:, j * 32:(j + 1) * 32],
                                    wdcast[0:32, j * 128:(j + 1) * 128], eyeb[0:32, 0:32])
                return i
            step("tensor", f)

            def f(s):
                for j in range(KB):
                    i = s.activation(wbct[:, j * 32:(j + 1) * 32], psSb[:, j * 32:(j + 1) * 32],
                                     AF.Copy, bias=0.0, scale=ca1p[:, j:j + 1])
                return i
            step("scalar", f)

            def f(t):
                for j in range(KB):
                    i = t.matmul(psT[0:32, 8:9], wbct[:, j * 32:(j + 1) * 32],
                                 saca[:, j:j + 1], start=(j == 0), stop=(j == KB - 1))
                return i
            step("tensor", f)
            step("vector", lambda v: v.tensor_tensor(bcb[:, 0:1], psT[0:32, 8:9],
                                                     bcb[:, 0:1], ALU.add))

        with contextlib.ExitStack() as _ps:
            wbig = _ps.enter_context(nc.sbuf_tensor("wbig", [128, 16 * D], F32))
            q32b = _ps.enter_context(nc.sbuf_tensor("q32b", [128, 1024], I32))
            wqb = _ps.enter_context(nc.sbuf_tensor("wqb", [128, 1024], BF16))
            wstage = _ps.enter_context(nc.sbuf_tensor("wstage", [128, KB * 128], BF16))
            _dummy_ = 0
            for half in range(2):
                step("sync", lambda sy, half=half: sy.dma_start(
                    out=wbig[:], in_=w1_h[half * 2048:(half + 1) * 2048, :]
                    .rearrange("(jt p) c -> p jt c", p=128)), n_dma=1)
                step("vector", lambda v, half=half: v.tensor_reduce(
                    gmx32[:, half:half + 1], wbig[:], AX, ALU.max,
                    apply_absolute_value=True))
            for rt in range(KB):
                step("sync", lambda sy, rt=rt: sy.dma_start(
                    out=wbig[:, 0:HID], in_=w2_h[rt * 128:(rt + 1) * 128, :]), n_dma=1)
                step("vector", lambda v, rt=rt: v.tensor_reduce(
                    gmx32[:, 2 + rt:3 + rt], wbig[:, 0:HID], AX, ALU.max,
                    apply_absolute_value=True))

            def f(v):
                vsp(v)
                v.tensor_reduce(gmaxacc[:, 2:3], gmx32[:, 0:2], AX, ALU.max)
                return v.tensor_reduce(gmaxacc[:, 3:4], gmx32[:, 2:2 + KB], AX, ALU.max)
            step("vector", f)

            for (col, s_s, s_r) in ((2, S_SW1, S_RSW1), (3, S_SW2, S_RSW2)):
                step("sync", lambda sy, col=col: sy.dma_start(
                    out=pb_d.ap(), in_=gmaxacc[:, col:col + 1]), n_dma=1)
                step("sync", lambda sy: sy.dma_start(out=srow[0:1, 0:128], in_=pb_d.ap()),
                     n_dma=1)

                def f(v, s_s=s_s, s_r=s_r):
                    v.tensor_reduce(sml[0:1, S_TMP:S_TMP + 1], srow[0:1, 0:128], AX, ALU.max)
                    vsp(v)
                    v.tensor_scalar_max(sml[0:1, S_TMP:S_TMP + 1],
                                        sml[0:1, S_TMP:S_TMP + 1], MIN_RANGE)
                    vsp(v)
                    v.tensor_scalar_mul(sml[0:1, s_s:s_s + 1], sml[0:1, S_TMP:S_TMP + 1],
                                        1.0 / QP)
                    vsp(v)
                    return v.reciprocal(sml[0:1, s_r:s_r + 1], sml[0:1, s_s:s_s + 1])
                step("vector", f)

            step("sync", lambda sy: sy.dma_start(out=sml_d.ap(), in_=sml[:]), n_dma=1)
            step("gpsimd", lambda g: g.dma_start(
                out=smlb[:], in_=sml_d.ap().unsqueeze(1).broadcast_to([1, 128, 16])),
                n_dma=1)

            for half in range(2):
                step("sync", lambda sy, half=half: sy.dma_start(
                    out=wbig[:], in_=w1_h[half * 2048:(half + 1) * 2048, :]
                    .rearrange("(jt p) c -> p jt c", p=128)), n_dma=1)
                for jt in range(16):
                    jb = half * 16 + jt
                    step("scalar", lambda s, jt=jt: s.activation(
                        q32b[:], wbig[:, jt * D:(jt + 1) * D], AF.Copy,
                        bias=0.0, scale=smlb[:, S_RSW1:S_RSW1 + 1]))
                    step("vector", lambda v: v.tensor_copy(wqb[:], q32b[:]))

                    def f(t):
                        for kb in range(KB):
                            ps = psSb if kb < 4 else psTb
                            i = t.transpose(ps[:, (kb % 4) * 128:(kb % 4 + 1) * 128],
                                            wqb[:, kb * 128:(kb + 1) * 128], eyeb[:, :])
                        return i
                    step("tensor", f)

                    def f(v):
                        v.tensor_copy(wstage[:, 0:512], psSb[:, 0:512])
                        return v.tensor_copy(wstage[:, 512:1024], psTb[:, 0:512])
                    step("vector", f)
                    step("sync", lambda sy, jb=jb: sy.dma_start(
                        out=w1qt_d.ap()[jb].rearrange("kb p j -> p kb j"),
                        in_=wstage[:]), n_dma=1)
            for rt in range(KB):
                step("sync", lambda sy, rt=rt: sy.dma_start(
                    out=wbig[:, 0:HID], in_=w2_h[rt * 128:(rt + 1) * 128, :]), n_dma=1)
                for g4 in range(4):
                    step("scalar", lambda s, g4=g4: s.activation(
                        q32b[:], wbig[:, g4 * 1024:(g4 + 1) * 1024], AF.Copy,
                        bias=0.0, scale=smlb[:, S_RSW2:S_RSW2 + 1]))
                    step("vector", lambda v: v.tensor_copy(wqb[:], q32b[:]))

                    def f(t):
                        for k in range(KB):
                            ps = psSb if k < 4 else psTb
                            i = t.transpose(ps[:, (k % 4) * 128:(k % 4 + 1) * 128],
                                            wqb[:, k * 128:(k + 1) * 128], eyeb[:, :])
                        return i
                    step("tensor", f)

                    def f(v):
                        v.tensor_copy(wstage[:, 0:512], psSb[:, 0:512])
                        return v.tensor_copy(wstage[:, 512:1024], psTb[:, 0:512])
                    step("vector", f)

                    def f(sy, rt=rt, g4=g4):
                        outs = []
                        for k in range(KB):
                            outs.append(sy.dma_start(
                                out=w2qt_d.ap()[rt, g4 * KB + k],
                                in_=wstage[:, k * 128:(k + 1) * 128]))
                        return outs
                    step("sync", f, n_dma=8)

        with contextlib.ExitStack() as _ps:
            xtmA = _ps.enter_context(nc.sbuf_tensor("xtmA", [128, 8 * D], BF16))
            xtmB = _ps.enter_context(nc.sbuf_tensor("xtmB", [128, 8 * D], BF16))
            xfm = _ps.enter_context(nc.sbuf_tensor("xfm", [128, KB * W], BF16))
            xsq0 = _ps.enter_context(nc.sbuf_tensor("xsq0", [128, W], BF16))
            xsq1 = _ps.enter_context(nc.sbuf_tensor("xsq1", [128, W], BF16))
            xc = _ps.enter_context(nc.sbuf_tensor("xc", [128, KB * W], BF16))
            rrep = _ps.enter_context(nc.sbuf_tensor("rrep", [128, W], BF16))
            mrrep = _ps.enter_context(nc.sbuf_tensor("mrrep", [128, W], BF16))
            reprow = _ps.enter_context(nc.sbuf_tensor("reprow", [1, 2 * W], BF16))
            sgb = _ps.enter_context(nc.sbuf_tensor("sgb", [128, EB * W], F32))
            lnt = _ps.enter_context(nc.sbuf_tensor("lnt", [128, EB * W], BF16))
            h1b0 = _ps.enter_context(nc.sbuf_tensor("h1b0", [128, W], BF16))
            h1b1 = _ps.enter_context(nc.sbuf_tensor("h1b1", [128, W], BF16))
            dxt = _ps.enter_context(nc.sbuf_tensor("dxt", [128, EB * W], BF16))
            bcrow = _ps.enter_context(nc.sbuf_tensor("bcrow", [32, W], BF16))
            bcr0 = _ps.enter_context(nc.sbuf_tensor("bcr0", [128, 2 * W], BF16))
            bcr1 = _ps.enter_context(nc.sbuf_tensor("bcr1", [128, 2 * W], BF16))
            ab0 = _ps.enter_context(nc.sbuf_tensor("ab0", [128, W], F32))
            ab1 = _ps.enter_context(nc.sbuf_tensor("ab1", [128, W], F32))
            ab2 = _ps.enter_context(nc.sbuf_tensor("ab2", [128, W], F32))
            bxb = _ps.enter_context(nc.sbuf_tensor("bxb", [128, W], F32))
            hbuf = _ps.enter_context(nc.sbuf_tensor("hbuf", [128, W], F32))
            hcb = _ps.enter_context(nc.sbuf_tensor("hcb", [128, W], BF16))
            yacc = _ps.enter_context(nc.sbuf_tensor("yacc", [128, EB * W], BF16))
            xpw = _ps.enter_context(nc.sbuf_tensor("xpw", [128, EB * W], BF16))
            state = _ps.enter_context(nc.sbuf_tensor("state", [128, 64], F32))
            _dummy_ = 0
            xtm = [xtmA, xtmB]
            xsq = [xsq0, xsq1]
            h1b = [h1b0, h1b1]
            bcr = [bcr0, bcr1]
            abb = [ab0, ab1, ab2]

            step("vector", lambda v: v.memset(state[:], 0.0))

            def prefetch(w):
                tkb = TK.v
                ldx.take(16)

                def body(g, w=w, tkb=tkb):
                    g.wait_ge(TK.h, tkb)
                    g.dma_start(out=xtm[w % 2][:],
                                in_=x_h[w * W:(w + 1) * W, :]
                                .rearrange("(tt p) c -> p tt c", p=128)).then_inc(ldx.h, 16)
                blk.gpsimd(body)
            prefetch(0)

            for w in range(NW):
                tw0 = w * W
                if w + 1 < NW:
                    prefetch(w + 1)

                def f(sy, w=w):
                    outs = []
                    for tt in range(8):
                        for kb in range(KB):
                            dst = xfm[:, kb * W + tt * 128:kb * W + (tt + 1) * 128]
                            outs.append(sy.dma_start_transpose(
                                dst, xtm[w % 2][:, tt * D + kb * 128:tt * D + (kb + 1) * 128]))
                    return outs
                step("sync", f, n_dma=64, extra_waits=[(ldx, 16 * (w + 1))])

                sq_sc0, sq_mm0 = sqsc.v, sqmm.v
                sqsc.take(KB)
                sqmm.take(KB)
                tkb = TK.v

                def fsq(s, base=sq_mm0, tkb=tkb):
                    s.wait_ge(TK.h, tkb)
                    for kb in range(KB):
                        if kb >= 2:
                            s.wait_ge(sqmm.h, base + kb - 1)
                        s.activation(xsq[kb % 2][:], xfm[:, kb * W:(kb + 1) * W],
                                     AF.Square, bias=0.0, scale=1.0).then_inc(sqsc.h, 1)
                blk.scalar(fsq)

                def fst(t, base=sq_sc0, tkb=tkb):
                    t.wait_ge(TK.h, tkb)
                    for kb in range(KB):
                        for hh in range(2):
                            t.matmul(psU[0:1, hh * 512:(hh + 1) * 512], onescol[:],
                                     xfm[:, kb * W + hh * 512:kb * W + (hh + 1) * 512],
                                     start=(kb == 0), stop=(kb == KB - 1))
                        t.wait_ge(sqsc.h, base + kb + 1)
                        for hh in range(2):
                            i = t.matmul(psV[0:1, hh * 512:(hh + 1) * 512], onescol[:],
                                         xsq[kb % 2][:, hh * 512:(hh + 1) * 512],
                                         start=(kb == 0), stop=(kb == KB - 1))
                        i.then_inc(sqmm.h, 1)
                blk.tensor(fst)

                def f(v):
                    v.tensor_scalar_mul(srow[0:1, 0:W], psU[0:1, 0:W], 1.0 / D)
                    v.tensor_scalar_mul(srow[0:1, W:2 * W], psV[0:1, 0:W], 1.0 / D)
                    v.tensor_tensor(srow[0:1, 2 * W:3 * W], srow[0:1, 0:W],
                                    srow[0:1, 0:W], ALU.mult)
                    return v.tensor_tensor(srow[0:1, W:2 * W], srow[0:1, W:2 * W],
                                           srow[0:1, 2 * W:3 * W], ALU.subtract)
                step("vector", f, extra_waits=[(sqmm, sqmm.v)])

                def f(s):
                    s.activation(srow[0:1, W:2 * W], srow[0:1, W:2 * W], AF.Ln,
                                 bias=epsv[0:1, 0:1], scale=1.0)
                    return s.activation(srow[0:1, W:2 * W], srow[0:1, W:2 * W], AF.Exp,
                                        bias=0.0, scale=-0.5)
                step("scalar", f)

                def f(v):
                    v.tensor_copy(reprow[0:1, 0:W], srow[0:1, W:2 * W])
                    return v.tensor_tensor(reprow[0:1, W:2 * W], srow[0:1, 0:W],
                                           srow[0:1, W:2 * W], ALU.mult)
                step("vector", f)

                def f(t):
                    for hh in range(2):
                        t.matmul(psU[:, hh * 512:(hh + 1) * 512], ones1[:],
                                 reprow[0:1, hh * 512:(hh + 1) * 512], start=True, stop=True)
                        i = t.matmul(psV[:, hh * 512:(hh + 1) * 512], ones1[:],
                                     reprow[0:1, W + hh * 512:W + (hh + 1) * 512],
                                     start=True, stop=True)
                    return i
                step("tensor", f)

                def f(v):
                    v.tensor_copy(rrep[:], psU[:, 0:W])
                    return v.tensor_copy(mrrep[:], psV[:, 0:W])
                step("vector", f)

                def f(v):
                    for kb in range(KB):
                        v.tensor_tensor(xc[:, kb * W:(kb + 1) * W], xfm[:, kb * W:(kb + 1) * W],
                                        rrep[:], ALU.mult)
                        i = v.tensor_tensor(xc[:, kb * W:(kb + 1) * W],
                                            xc[:, kb * W:(kb + 1) * W], mrrep[:], ALU.subtract)
                    return i
                step("vector", f)
                if w == 0:
                    dbg_out("d_xc", xc[:, 0:2048])

                dp_mm0, dp_sc0 = dpmm.v, dpsc.v
                dpmm.take(EB)
                dpsc.take(EB)
                tkb = TK.v

                def fdp(t, base=dp_sc0, tkb=tkb):
                    t.wait_ge(TK.h, tkb)
                    for eb in range(EB):
                        if eb >= 2:
                            t.wait_ge(dpsc.h, base + eb - 1)
                        ps = psU if eb % 2 == 0 else psV
                        for j in range(KB):
                            for hh in range(2):
                                i = t.matmul(ps[:, hh * 512:(hh + 1) * 512],
                                             wdt[:, j * EHALF + eb * 128:j * EHALF + (eb + 1) * 128],
                                             xc[:, j * W + hh * 512:j * W + (hh + 1) * 512],
                                             start=(j == 0), stop=(j == KB - 1))
                        i.then_inc(dpmm.h, 1)
                blk.tensor(fdp)

                def fsg(s, base=dp_mm0, tkb=tkb):
                    s.wait_ge(TK.h, tkb)
                    for eb in range(EB):
                        s.wait_ge(dpmm.h, base + eb + 1)
                        ps = psU if eb % 2 == 0 else psV
                        s.activation(sgb[:, eb * W:(eb + 1) * W], ps[:, 0:W], AF.Sigmoid,
                                     bias=negc1[:, eb:eb + 1], scale=-1.0).then_inc(dpsc.h, 1)
                blk.scalar(fsg)

                def f(s):
                    for eb in range(EB):
                        i = s.activation(lnt[:, eb * W:(eb + 1) * W], sgb[:, eb * W:(eb + 1) * W],
                                         AF.Ln, bias=0.0, scale=1.0)
                    return i
                step("scalar", f, extra_waits=[(dpsc, dpsc.v)])
                if w == 0:
                    dbg_out("d_lnt", lnt[:, 0:2048])

                def f(t):
                    for j in range(KB):
                        for hh in range(2):
                            i = t.matmul(psU[0:32, hh * 512:(hh + 1) * 512],
                                         wbct[:, j * 32:(j + 1) * 32],
                                         xc[:, j * W + hh * 512:j * W + (hh + 1) * 512],
                                         start=(j == 0), stop=(j == KB - 1))
                    return i
                step("tensor", f)
                step("vector", lambda v: v.tensor_scalar(bcrow[:], psU[0:32, 0:W],
                                                         bcb[:, 0:1], None, ALU.add))
                if w == 0:
                    dbg_out("d_bc", bcrow[0:32, 0:W])

                def f(sy, w=w):
                    dst = bcq_d.ap()[w % 2].rearrange("(n two) c -> two n c", two=2)
                    return [sy.dma_start(out=dst[0], in_=bcrow[0:16, :]),
                            sy.dma_start(out=dst[1], in_=bcrow[16:32, :])]
                step("sync", f, n_dma=2)

                def fhd(v):
                    for eb in range(EB):
                        v.tensor_scalar(h1b[eb % 2][:], xc[:, eb * W:(eb + 1) * W],
                                        ca1p[:, eb:eb + 1], sap[:, eb:eb + 1],
                                        ALU.mult, ALU.add)
                        i = v.tensor_tensor(dxt[:, eb * W:(eb + 1) * W],
                                            lnt[:, eb * W:(eb + 1) * W],
                                            h1b[eb % 2][:], ALU.mult)
                    return i
                step("vector", fhd)

                bcs0, sA0, vA0, vN0 = bcs.v, sA.v, vA.v, vN.v
                bcs.take(16 * 16)
                sA.take(16 * EB)
                vA.take(16 * EB)
                vN.take(16)
                tkb = TK.v

                def fbc(g, w=w, v0=vN0, tkb=tkb):
                    g.wait_ge(TK.h, tkb)
                    for n in range(16):
                        if n >= 2:
                            g.wait_ge(vN.h, v0 + n - 1)
                        src = bcq_d.ap()[w % 2][2 * n:2 * n + 2, :]
                        src = src.unsqueeze(0).broadcast_to([128, 2, W])
                        g.dma_start(out=bcr[n % 2][:], in_=src).then_inc(bcs.h, 16)
                blk.gpsimd(fbc)

                def fab(s, v0=vA0, tkb=tkb):
                    s.wait_ge(TK.h, tkb)
                    for n in range(16):
                        for eb in range(EB):
                            idx = n * EB + eb
                            if idx >= 3:
                                s.wait_ge(vA.h, v0 + idx - 2)
                            s.activation(abb[idx % 3][:], lnt[:, eb * W:(eb + 1) * W], AF.Exp,
                                         bias=0.0,
                                         scale=negA[:, eb * DS + n:eb * DS + n + 1]
                                         ).then_inc(sA.h, 1)
                blk.scalar(fab)

                def fscan(v, b0=bcs0, s0=sA0, n0=vN0, tkb=tkb):
                    v.wait_ge(TK.h, tkb)
                    for n in range(16):
                        v.wait_ge(bcs.h, b0 + 16 * (n + 1))
                        for eb in range(EB):
                            idx = n * EB + eb
                            v.wait_ge(sA.h, s0 + idx + 1)
                            v.tensor_tensor(bxb[:], dxt[:, eb * W:(eb + 1) * W],
                                            bcr[n % 2][:, 0:W], ALU.mult)
                            v.tensor_tensor_scan(hbuf[:], abb[idx % 3][:], bxb[:],
                                                 state[:, idx:idx + 1], ALU.mult, ALU.add)
                            if n == 0:
                                i1 = v.tensor_tensor(yacc[:, eb * W:(eb + 1) * W],
                                                     hbuf[:], bcr[n % 2][:, W:2 * W],
                                                     ALU.mult)
                            else:
                                v.tensor_tensor(hcb[:], hbuf[:],
                                                bcr[n % 2][:, W:2 * W], ALU.mult)
                                i1 = v.tensor_tensor(yacc[:, eb * W:(eb + 1) * W],
                                                     yacc[:, eb * W:(eb + 1) * W],
                                                     hcb[:], ALU.add)
                            if eb == EB - 1:
                                i1.then_inc(vN.h, 1)
                            v.tensor_scalar_add(state[:, idx:idx + 1],
                                                hbuf[:, W - 1:W],
                                                0.0).then_inc(vA.h, 1)
                blk.vector(fscan)

                step("vector", lambda v: v.nop(), extra_waits=[(vA, vA.v), (vN, vN.v)])
                if w == 0:
                    dbg_out("d_y", yacc[:, 0:2048])

                def f(v):
                    for eb in range(EB):
                        i = v.scalar_tensor_tensor(xpw[:, eb * W:(eb + 1) * W],
                                                   yacc[:, eb * W:(eb + 1) * W],
                                                   ganeg[:, eb:eb + 1],
                                                   xfm[:, eb * W:(eb + 1) * W],
                                                   ALU.mult, ALU.add)
                    return i
                step("vector", f)

                def f(sy, tw0=tw0):
                    return [sy.dma_start(out=xps_d.ap()[eb][:, tw0:tw0 + W],
                                         in_=xpw[:, eb * W:(eb + 1) * W])
                            for eb in range(EB)]
                step("sync", f, n_dma=4)

        mix_ps.close()

        def f(g):
            return [g.dma_start(out=xpsh_d.ap()[eb],
                                in_=xps_d.ap()[eb][:, REGS["gds_invT"]])
                    for eb in range(EB)]
        step("gpsimd", f, n_dma=4)

        tkb = TK.v
        TK.take(1)
        ccs.take(1)

        def f(g, tkb=tkb, cval=ccs.v):
            g.wait_ge(TK.h, tkb)
            g.collective_compute(
                "AllGather", ALU.bypass,
                replica_groups=[[0, 1], [2, 3], [4, 5], [6, 7]],
                ins=[xpsh_d.ap().opt()],
                outs=[xch_d.ap().opt()]).then_inc(ccs.h, 1)
            g.wait_ge(ccs.h, cval)
            g.nop().then_inc(TK.h, 1)
        blk.gpsimd(f)

        with nc.sbuf_tensor("xp_fm", [128, KB * THL], BF16) as xp_fm:
            with contextlib.ExitStack() as _ps:
                r2rep = _ps.enter_context(nc.sbuf_tensor("r2rep", [128, THL], BF16))
                mr2rep = _ps.enter_context(nc.sbuf_tensor("mr2rep", [128, THL], BF16))
                reprow2 = _ps.enter_context(nc.sbuf_tensor("reprow2", [1, 2 * W], BF16))
                z2m = _ps.enter_context(nc.sbuf_tensor("z2m", [128, KB * THL], BF16))
                q32c = _ps.enter_context(nc.sbuf_tensor("q32c", [128, 2048], I32))
                w1s0 = _ps.enter_context(nc.sbuf_tensor("w1s0", [128, KB * 128], BF16))
                w1s1 = _ps.enter_context(nc.sbuf_tensor("w1s1", [128, KB * 128], BF16))
                gbf0 = _ps.enter_context(nc.sbuf_tensor("gbf0", [128, THL], BF16))
                gbf1 = _ps.enter_context(nc.sbuf_tensor("gbf1", [128, THL], BF16))
                _dummy_ = 0
                w1s = [w1s0, w1s1]
                gbf = [gbf0, gbf1]

                def f(g):
                    return [g.dma_start(out=xp_fm[:, eb * THL:(eb + 1) * THL],
                                        in_=xps_d.ap()[eb][:, REGS["gds_thT"]])
                            for eb in range(EB)]
                step("gpsimd", f, n_dma=4)

                def f(g):
                    return [g.dma_start(
                        out=xp_fm[:, (EB + eb) * THL:(EB + eb + 1) * THL],
                        in_=xch_d.ap()[REGS["gds_ag"]][0][eb])
                        for eb in range(EB)]
                step("gpsimd", f, n_dma=4)
                dbg_out("d_xp", xp_fm[:, 0:2048])

                for tc in range(4):
                    sq_sc0, sq_mm0 = sqsc.v, sqmm.v
                    sqsc.take(KB)
                    sqmm.take(KB)
                    tkb = TK.v

                    def fsq(s, tc=tc, base=sq_mm0, tkb=tkb):
                        s.wait_ge(TK.h, tkb)
                        for kb in range(KB):
                            if kb >= 2:
                                s.wait_ge(sqmm.h, base + kb - 1)
                            xsq2 = gbf[kb % 2][:, 0:W]
                            s.activation(xsq2,
                                         xp_fm[:, kb * THL + tc * W:kb * THL + (tc + 1) * W],
                                         AF.Square, bias=0.0, scale=1.0).then_inc(sqsc.h, 1)
                    blk.scalar(fsq)

                    def fst(t, tc=tc, base=sq_sc0, tkb=tkb):
                        t.wait_ge(TK.h, tkb)
                        for kb in range(KB):
                            for hh in range(2):
                                t.matmul(psU[0:1, hh * 512:(hh + 1) * 512], onescol[:],
                                         xp_fm[:, kb * THL + tc * W + hh * 512:
                                                kb * THL + tc * W + (hh + 1) * 512],
                                         start=(kb == 0), stop=(kb == KB - 1))
                            t.wait_ge(sqsc.h, base + kb + 1)
                            xsq2 = gbf[kb % 2][:, 0:W]
                            for hh in range(2):
                                i = t.matmul(psV[0:1, hh * 512:(hh + 1) * 512], onescol[:],
                                             xsq2[:, hh * 512:(hh + 1) * 512],
                                             start=(kb == 0), stop=(kb == KB - 1))
                            i.then_inc(sqmm.h, 1)
                    blk.tensor(fst)

                    def f(v):
                        v.tensor_scalar_mul(srow[0:1, 0:W], psU[0:1, 0:W], 1.0 / D)
                        v.tensor_scalar_mul(srow[0:1, W:2 * W], psV[0:1, 0:W], 1.0 / D)
                        v.tensor_tensor(srow[0:1, 2 * W:3 * W], srow[0:1, 0:W],
                                        srow[0:1, 0:W], ALU.mult)
                        return v.tensor_tensor(srow[0:1, W:2 * W], srow[0:1, W:2 * W],
                                               srow[0:1, 2 * W:3 * W], ALU.subtract)
                    step("vector", f, extra_waits=[(sqmm, sqmm.v)])

                    def f(s):
                        s.activation(srow[0:1, W:2 * W], srow[0:1, W:2 * W], AF.Ln,
                                     bias=epsv[0:1, 0:1], scale=1.0)
                        return s.activation(srow[0:1, W:2 * W], srow[0:1, W:2 * W], AF.Exp,
                                            bias=0.0, scale=-0.5)
                    step("scalar", f)

                    def f(v):
                        v.tensor_copy(reprow2[0:1, 0:W], srow[0:1, W:2 * W])
                        return v.tensor_tensor(reprow2[0:1, W:2 * W], srow[0:1, 0:W],
                                               srow[0:1, W:2 * W], ALU.mult)
                    step("vector", f)

                    def f(t):
                        for hh in range(2):
                            t.matmul(psU[:, hh * 512:(hh + 1) * 512], ones1[:],
                                     reprow2[0:1, hh * 512:(hh + 1) * 512],
                                     start=True, stop=True)
                            i = t.matmul(psV[:, hh * 512:(hh + 1) * 512], ones1[:],
                                         reprow2[0:1, W + hh * 512:W + (hh + 1) * 512],
                                         start=True, stop=True)
                        return i
                    step("tensor", f)

                    def f(v, tc=tc):
                        v.tensor_copy(r2rep[:, tc * W:(tc + 1) * W], psU[:, 0:W])
                        return v.tensor_copy(mr2rep[:, tc * W:(tc + 1) * W], psV[:, 0:W])
                    step("vector", f)

                def f(v):
                    for kb in range(KB):
                        v.tensor_tensor(z2m[:, kb * THL:(kb + 1) * THL],
                                        xp_fm[:, kb * THL:(kb + 1) * THL], r2rep[:], ALU.mult)
                        v.tensor_tensor(z2m[:, kb * THL:(kb + 1) * THL],
                                        z2m[:, kb * THL:(kb + 1) * THL], mr2rep[:],
                                        ALU.subtract)
                        i = v.tensor_scalar(z2m[:, kb * THL:(kb + 1) * THL],
                                            z2m[:, kb * THL:(kb + 1) * THL],
                                            cb1p[:, kb:kb + 1], sbp[:, kb:kb + 1],
                                            ALU.mult, ALU.add)
                    return i
                step("vector", f)
                dbg_out("d_z2", z2m[:, 0:2048])

                def f(v):
                    for kb in range(KB):
                        v.tensor_reduce(gmx32[:, kb:kb + 1],
                                        z2m[:, kb * THL:(kb + 1) * THL], AX, ALU.max,
                                        apply_absolute_value=True)
                    vsp(v)
                    return v.tensor_reduce(gmaxacc[:, 1:2], gmx32[:, 0:KB], AX, ALU.max)
                step("vector", f)

                step("sync", lambda sy: sy.dma_start(out=pb_d.ap(), in_=gmaxacc[:, 1:2]),
                     n_dma=1)
                step("sync", lambda sy: sy.dma_start(out=srow[0:1, 0:128], in_=pb_d.ap()),
                     n_dma=1)

                def f(v):
                    v.tensor_reduce(sml[0:1, S_TMP:S_TMP + 1], srow[0:1, 0:128], AX, ALU.max)
                    vsp(v)
                    return v.tensor_copy(sml[0:1, S_SX:S_SX + 1], sml[0:1, S_TMP:S_TMP + 1])
                step("vector", f)
                step("sync", lambda sy: sy.dma_start(out=mx1i_d.ap(), in_=sml[0:1, 0:8]),
                     n_dma=1)

                tkb = TK.v
                TK.take(1)
                ccs.take(1)

                def f(g, tkb=tkb, cval=ccs.v):
                    g.wait_ge(TK.h, tkb)
                    g.collective_compute(
                        "AllReduce", ALU.max, replica_groups=[list(range(8))],
                        ins=[mx1i_d.ap().opt()], outs=[mx1o_d.ap().opt()]).then_inc(ccs.h, 1)
                    g.wait_ge(ccs.h, cval)
                    g.nop().then_inc(TK.h, 1)
                blk.gpsimd(f)

                step("sync", lambda sy: sy.dma_start(out=srow[0:1, 0:8], in_=mx1o_d.ap()),
                     n_dma=1)

                def f(v):
                    v.tensor_scalar_max(sml[0:1, S_TMP:S_TMP + 1], srow[0:1, 1:2], MIN_RANGE)
                    vsp(v)
                    v.tensor_scalar_mul(sml[0:1, S_SX:S_SX + 1], sml[0:1, S_TMP:S_TMP + 1], 1.0 / QP)
                    vsp(v)
                    v.reciprocal(sml[0:1, S_RSX:S_RSX + 1], sml[0:1, S_SX:S_SX + 1])
                    return v.tensor_tensor(sml[0:1, S_S1:S_S1 + 1], sml[0:1, S_SW1:S_SW1 + 1],
                                           sml[0:1, S_SX:S_SX + 1], ALU.mult)
                step("vector", f)
                step("sync", lambda sy: sy.dma_start(out=sml_d.ap(), in_=sml[:]), n_dma=1)
                step("gpsimd", lambda g: g.dma_start(
                    out=smlb[:], in_=sml_d.ap().unsqueeze(1).broadcast_to([1, 128, 16])),
                    n_dma=1)
                step("vector", lambda v: v.tensor_scalar(b1s1[:], b1sb[:],
                                                         smlb[:, S_S1:S_S1 + 1], None, ALU.mult))

                for kb in range(KB):
                    for hf in range(2):
                        c0 = kb * THL + hf * 2048
                        step("scalar", lambda s, c0=c0: s.activation(
                            q32c[:], z2m[:, c0:c0 + 2048], AF.Copy, bias=0.0,
                            scale=smlb[:, S_RSX:S_RSX + 1]))
                        step("vector", lambda v, c0=c0:
                             v.tensor_copy(z2m[:, c0:c0 + 2048], q32c[:]))
                dbg_out("d_xq", z2m[:, 0:2048])

                ftr0, fmm0, fsc0, fmx0, fsp0 = ftr.v, fmm.v, fsc.v, fmx.v, fsp.v
                ftr.take(16 * 32)
                fmm.take(32 * 4)
                fsc.take(32 * 4)
                fmx.take(33)
                fsp.take(16 * 32)
                tkb = TK.v

                def fw1(sy, t0=ftr0, m0=fmm0, x0=fmx0, tkb=tkb):
                    sy.wait_ge(TK.h, tkb)
                    for jb in range(34):
                        if jb < 32:
                            if jb >= 2:
                                sy.wait_ge(fmm.h, m0 + 4 * (jb - 1))
                            sy.dma_start(out=w1s[jb % 2][:],
                                         in_=w1qt_d.ap()[jb].rearrange("kb p j -> p kb j")
                                         ).then_inc(ftr.h, 16)
                        if jb >= 2:
                            sjb = jb - 2
                            sy.wait_ge(fmx.h, x0 + sjb + 1)
                            sy.dma_start(out=gsp_d.ap()[sjb],
                                         in_=gbf[sjb % 2][:]).then_inc(fsp.h, 16)
                blk.sync(fw1)

                def fm1(t, t0=ftr0, s0=fsc0, tkb=tkb):
                    t.wait_ge(TK.h, tkb)
                    for jb in range(32):
                        t.wait_ge(ftr.h, t0 + 16 * (jb + 1))
                        for tw in range(4):
                            gi = jb * 4 + tw
                            if gi >= 2:
                                t.wait_ge(fsc.h, s0 + gi - 1)
                            ps = psU if gi % 2 == 0 else psV
                            for kb in range(KB):
                                for hh in range(2):
                                    i = t.matmul(
                                        ps[:, hh * 512:(hh + 1) * 512],
                                        w1s[jb % 2][:, kb * 128:(kb + 1) * 128],
                                        z2m[:, kb * THL + tw * 1024 + hh * 512:
                                            kb * THL + tw * 1024 + (hh + 1) * 512],
                                        start=(kb == 0), stop=(kb == KB - 1))
                            i.then_inc(fmm.h, 1)
                blk.tensor(fm1)

                def fg1(s, m0=fmm0, p0=fsp0, tkb=tkb):
                    s.wait_ge(TK.h, tkb)
                    for jb in range(32):
                        for tw in range(4):
                            gi = jb * 4 + tw
                            if tw == 0 and jb >= 2:
                                s.wait_ge(fsp.h, p0 + 16 * (jb - 1))
                            s.wait_ge(fmm.h, m0 + gi + 1)
                            ps = psU if gi % 2 == 0 else psV
                            s.activation(gbf[jb % 2][:, tw * 1024:(tw + 1) * 1024],
                                         ps[:, 0:1024], AF.Gelu, bias=b1s1[:, jb:jb + 1],
                                         scale=smlb[:, S_S1:S_S1 + 1]).then_inc(fsc.h, 1)
                blk.scalar(fg1)

                def fx1(v, s0=fsc0, tkb=tkb):
                    v.wait_ge(TK.h, tkb)
                    for jb in range(32):
                        v.wait_ge(fsc.h, s0 + 4 * (jb + 1))
                        v.tensor_reduce(gmx32[:, jb:jb + 1], gbf[jb % 2][:], AX, ALU.max,
                                        apply_absolute_value=True).then_inc(fmx.h, 1)
                    vsp(v)
                    v.tensor_reduce(gmaxacc[:, 2:3], gmx32[:], AX,
                                    ALU.max).then_inc(fmx.h, 1)
                blk.vector(fx1)



                step("vector", lambda v: v.nop(),
                     extra_waits=[(fsp, fsp.v), (fmx, fmx.v), (fsc, fsc.v)])
                dbg_out("d_g", gbf[1][:, 0:2048])

                step("sync", lambda sy: sy.dma_start(out=pb_d.ap(), in_=gmaxacc[:, 2:3]),
                     n_dma=1)
                step("sync", lambda sy: sy.dma_start(out=srow[0:1, 0:128], in_=pb_d.ap()),
                     n_dma=1)

                def f(v):
                    v.tensor_reduce(sml[0:1, S_TMP:S_TMP + 1], srow[0:1, 0:128], AX, ALU.max)
                    vsp(v)
                    return v.tensor_copy(sml[0:1, S_S2:S_S2 + 1], sml[0:1, S_TMP:S_TMP + 1])
                step("vector", f)
                step("sync", lambda sy: sy.dma_start(out=mx2i_d.ap(), in_=sml[0:1, 0:8]),
                     n_dma=1)

                tkb = TK.v
                TK.take(1)
                ccs.take(1)

                def f(g, tkb=tkb, cval=ccs.v):
                    g.wait_ge(TK.h, tkb)
                    g.collective_compute(
                        "AllReduce", ALU.max, replica_groups=[list(range(8))],
                        ins=[mx2i_d.ap().opt()], outs=[mx2o_d.ap().opt()]).then_inc(ccs.h, 1)
                    g.wait_ge(ccs.h, cval)
                    g.nop().then_inc(TK.h, 1)
                blk.gpsimd(f)

                step("sync", lambda sy: sy.dma_start(out=srow[0:1, 0:8], in_=mx2o_d.ap()),
                     n_dma=1)

                def f(v):
                    v.tensor_scalar_max(sml[0:1, S_TMP:S_TMP + 1], srow[0:1, 0:1], MIN_RANGE)
                    vsp(v)
                    v.tensor_scalar_mul(sml[0:1, S_S2:S_S2 + 1], sml[0:1, S_TMP:S_TMP + 1], 1.0 / QP)
                    vsp(v)
                    v.reciprocal(sml[0:1, S_RS2:S_RS2 + 1], sml[0:1, S_S2:S_S2 + 1])
                    return v.tensor_tensor(sml[0:1, S_S2B:S_S2B + 1], sml[0:1, S_SW2:S_SW2 + 1],
                                           sml[0:1, S_S2:S_S2 + 1], ALU.mult)
                step("vector", f)
                step("sync", lambda sy: sy.dma_start(out=sml_d.ap(), in_=sml[:]), n_dma=1)
                step("gpsimd", lambda g: g.dma_start(
                    out=smlb[:], in_=sml_d.ap().unsqueeze(1).broadcast_to([1, 128, 16])),
                    n_dma=1)

                def f(v):
                    v.tensor_scalar(sgbrep[:], gbrep[:], smlb[:, S_S2B:S_S2B + 1],
                                    None, ALU.mult)
                    v.tensor_scalar(srow[0:1, 0:D], b2row[:], sml[0:1, S_S2B:S_S2B + 1],
                                    None, ALU.mult)
                    v.tensor_tensor(srow[0:1, 0:D], srow[0:1, 0:D], gbrow[:], ALU.mult)
                    return v.tensor_copy(reprow2[0:1, 0:D], srow[0:1, 0:D])
                step("vector", f)
                def f(t):
                    for hh in range(2):
                        i = t.matmul(psU[:, hh * 512:(hh + 1) * 512], ones1[:],
                                     reprow2[0:1, hh * 512:(hh + 1) * 512],
                                     start=True, stop=True)
                    return i
                step("tensor", f)
                step("vector", lambda v: v.tensor_copy(c2rep[:], psU[:, 0:1024]))

            with contextlib.ExitStack() as _ps:
                w2sb = _ps.enter_context(nc.sbuf_tensor("w2sb", [128, 32 * 1024], BF16))
                grl = _ps.enter_context(nc.sbuf_tensor("grl", [128, 32 * 512], BF16))
                q32d = _ps.enter_context(nc.sbuf_tensor("q32d", [128, 2048], I32))
                xtc0 = _ps.enter_context(nc.sbuf_tensor("xtc0", [128, 1024], BF16))
                xtc1 = _ps.enter_context(nc.sbuf_tensor("xtc1", [128, 1024], BF16))
                ebuf = _ps.enter_context(nc.sbuf_tensor("ebuf", [128, 1024], BF16))
                obf0 = _ps.enter_context(nc.sbuf_tensor("obf0", [128, 1024], F32))
                obf1 = _ps.enter_context(nc.sbuf_tensor("obf1", [128, 1024], F32))
                _dummy_ = 0
                xtc = [xtc0, xtc1]
                obf = [obf0, obf1]

                def f(sy):
                    outs = []
                    w2v = w2sb[:].rearrange("p (jb c) -> p jb c", jb=32)
                    for rt in range(KB):
                        outs.append(sy.dma_start(
                            out=w2v[:, :, rt * 128:(rt + 1) * 128],
                            in_=w2qt_d.ap()[rt].rearrange("jb j c -> j jb c")))
                    return outs
                step("sync", f, n_dma=8)

                for tw in range(8):
                    step("sync", lambda sy, tw=tw: sy.dma_start(
                        out=grl[:],
                        in_=gsp_d.ap()[:, :, tw * 512:(tw + 1) * 512]
                        .rearrange("jb p t -> p jb t")), n_dma=1)
                    for ch in range(8):
                        c0 = ch * 2048
                        step("scalar", lambda s, c0=c0: s.activation(
                            q32d[:], grl[:, c0:c0 + 2048], AF.Copy, bias=0.0,
                            scale=smlb[:, S_RS2:S_RS2 + 1]))
                        step("vector", lambda v, c0=c0:
                             v.tensor_copy(grl[:, c0:c0 + 2048], q32d[:]))

                    for tc in range(4):
                        def f(sy, tw=tw, tc=tc):
                            outs = []
                            for kb in range(KB):
                                dst = xtc[tc % 2][:, kb * 128:(kb + 1) * 128]
                                outs.append(sy.dma_start_transpose(
                                    dst, xp_fm[:, kb * THL + tw * 512 + tc * 128:
                                               kb * THL + tw * 512 + (tc + 1) * 128]))
                            return outs
                        step("sync", f, n_dma=8)

                        def f(t, tc=tc):
                            ps = psU if tc % 2 == 0 else psV
                            for jb in range(32):
                                for hh in range(2):
                                    i = t.matmul(
                                        ps[:, hh * 512:(hh + 1) * 512],
                                        grl[:, jb * 512 + tc * 128:jb * 512 + (tc + 1) * 128],
                                        w2sb[:, jb * 1024 + hh * 512:jb * 1024 + (hh + 1) * 512],
                                        start=(jb == 0), stop=(jb == 31))
                            return i
                        step("tensor", f)

                        def f(v, tc=tc):
                            ps = psU if tc % 2 == 0 else psV
                            v.tensor_tensor(ebuf[:], ps[:, 0:1024], sgbrep[:], ALU.mult)
                            v.tensor_tensor(ebuf[:], ebuf[:], c2rep[:], ALU.add)
                            return v.tensor_tensor(obf[tc % 2][:], ebuf[:], xtc[tc % 2][:],
                                                   ALU.add)
                        step("vector", f)

                        step("sync", lambda sy, tw=tw, tc=tc: sy.dma_start(
                            out=out_h[tw * 512 + tc * 128:tw * 512 + (tc + 1) * 128, :],
                            in_=obf[tc % 2][:]), n_dma=1)

        fin = TK.v

        def f(g, fin=fin):
            g.wait_ge(TK.h, fin)
        blk.gpsimd(f)

        def f(sy, fin=fin):
            sy.wait_ge(TK.h, fin)
        blk.sync(f)

    return nc


def _in_maps(inputs):
    x = np.ascontiguousarray(np.asarray(inputs["L0_attn_x"], dtype=np.float32))
    A = np.asarray(inputs["L0_attn_A"], dtype=np.float32)
    Wd = np.asarray(inputs["L0_attn_Wd"], dtype=np.float32)
    bd = np.asarray(inputs["L0_attn_bd"], dtype=np.float32)
    Wb = np.asarray(inputs["L0_attn_Wb"], dtype=np.float32)
    bb = np.asarray(inputs["L0_attn_bb"], dtype=np.float32)
    Wc = np.asarray(inputs["L0_attn_Wc"], dtype=np.float32)
    bc = np.asarray(inputs["L0_attn_bc"], dtype=np.float32)
    w1 = np.asarray(inputs["L0_ffn_w1"], dtype=np.float32)
    b1 = np.asarray(inputs["L0_ffn_b1"], dtype=np.float32)
    w2 = np.asarray(inputs["L0_ffn_w2"], dtype=np.float32)
    b2 = np.asarray(inputs["L0_ffn_b2"], dtype=np.float32)
    cc = np.asarray(inputs["L0_c"], dtype=np.float32)
    wmod = np.asarray(inputs["L0_wmod"], dtype=np.float32)

    wmod_re = np.ascontiguousarray(
        wmod.reshape(KB, 128, 12, 512).transpose(2, 0, 1, 3))
    eye = np.eye(128, dtype=np.float32)
    wbc_full = np.concatenate([Wb, Wc], axis=0)
    bbc_full = np.concatenate([bb, bc], axis=0)

    maps = []
    for c in range(8):
        b, eh = c // 2, c % 2
        permd = np.r_[eh * 512:(eh + 1) * 512, (1 - eh) * 512:(2 - eh) * 512]
        esl = slice(eh * 512, (eh + 1) * 512)
        cfg = np.zeros((1, 16), np.int32)
        for kb in range(8):
            cfg[0, kb] = (kb - 4 * eh) % 8
        cfg[0, 8] = eh
        cfg[0, 9] = 1 - eh
        cfg[0, 11] = eh
        cfg[0, 12] = 1 - eh
        maps.append({
            "x": np.ascontiguousarray(x[b][:, permd]),
            "wd": np.ascontiguousarray(Wd[esl][:, permd]),
            "wbc": np.ascontiguousarray(wbc_full[:, permd]),
            "asl": np.ascontiguousarray(A[esl]),
            "bdsl": np.ascontiguousarray(bd[esl]),
            "bbc": np.ascontiguousarray(bbc_full),
            "w1": np.ascontiguousarray(w1[:, permd]),
            "b1": np.ascontiguousarray(b1),
            "w2": np.ascontiguousarray(w2[permd, :]),
            "b2": np.ascontiguousarray(b2[permd]),
            "wmod": wmod_re,
            "crow": np.ascontiguousarray(cc[b:b + 1]),
            "eye": eye,
            "cfg": cfg,
        })
    return maps


def kernel(**inputs):
    nc = build()
    maps = _in_maps(inputs)
    res = run_bass_kernel_spmd(nc, maps, core_ids=list(range(8)))
    out = np.empty((B, L, D), np.float32)
    for c in range(8):
        b, th, eh = c // 2, c % 2, c % 2
        permd = np.r_[eh * 512:(eh + 1) * 512, (1 - eh) * 512:(2 - eh) * 512]
        out[b, th * THL:(th + 1) * THL] = res.results[c]["out"][:, permd]
    return out
